# version 02
# baseline (speedup 1.0000x reference)
import numpy as np
from contextlib import ExitStack
import concourse.bass as bass
import concourse.mybir as mybir
from concourse.bass_utils import run_bass_kernel_spmd

F32 = mybir.dt.float32
BF16 = mybir.dt.bfloat16
I32 = mybir.dt.int32
AF = mybir.ActivationFunctionType
ALU = mybir.AluOpType

NCORES = 8
TC = 2048
TH = TC + 2
D = 1024
DG = 8
NS = 16
FF = 3072
LNEPS = 1e-5

DEBUG = False
PHASES = 8
SCANPART = 5
PH6PART = 4

TILES_TC = [(0, 512), (512, 1024), (1024, 1536), (1536, 2048)]
TILES_TH = [(i * 410, (i + 1) * 410) for i in range(5)]

SMALL_LAYOUT = [
    ("w1_b", DG), ("v1_b", DG), ("w2_b", DG), ("dt_b", DG),
    ("ff2_b", DG), ("ff1_b", 24),
    ("ln1_g", DG), ("ln1_b", DG), ("ln_g", DG), ("ln_b", DG),
    ("ln2_g", DG), ("ln2_b", DG),
    ("conv", 3 * DG), ("D_p", DG),
    ("A", DG * NS),
    ("mask", 2), ("sel", 8), ("onecol", 1), ("eps", 1),
    ("ccin", 32), ("seeds", NS), ("Rbuf", NS),
]
SMAP = {}
_o = 0
for _nm, _w in SMALL_LAYOUT:
    SMAP[_nm] = _o
    _o += _w
SMALL_TOT = _o
NSMALL = SMAP["ccin"]


class Sched:
    ENGINES = ("sync", "vector", "scalar", "tensor", "gpsimd")

    def __init__(self):
        self.q = {e: [] for e in self.ENGINES}

    def on(self, eng, fn):
        self.q[eng].append(fn)


class Sem:
    def __init__(self, nc, es, name):
        self.h = es.enter_context(nc.semaphore(name))
        self.n = 0

    def bump(self, amt):
        self.n += amt
        return self.n


def build_program(debug=DEBUG, phases=None):
    if phases is None:
        phases = PHASES
    nc = bass.Bass("TRN2", target_bir_lowering=False, debug=False)
    es = ExitStack()
    sched = Sched()

    def din(name, shape, dt):
        return nc.dram_tensor(name, shape, dt, kind="ExternalInput").ap()

    x_tm = din("x_tm", [TH, D], F32)
    W = {}
    for nm, shape, dt in [
        ("w1_w", [D, D], F32), ("v1_w", [D, D], F32), ("w2_w", [D, D], F32),
        ("dt_w", [D, D], F32),
        ("Bp_w", [D, NS], F32), ("Cp_w", [D, NS], F32),
        ("ff1_w", [D, FF], F32), ("ff2_w", [FF, D], F32),
        ("wg", [FF, D], I32), ("wu", [FF, D], I32), ("wd", [D, FF], I32),
    ]:
        W[nm] = din(nm, shape, dt)
    smalls_in = din("smalls", [128, NSMALL], F32)
    bpcb_in = din("bpcb", [16, 2], F32)
    ident_f = din("ident_f", [128, 128], F32)
    ident_b = din("ident_b", [128, 128], BF16)
    oneover = din("oneover", [128, 1], BF16)
    nsel_in = din("nsel", [NS, NS * 128], BF16)

    y_out = nc.dram_tensor("y_out", [TC, D], F32, kind="ExternalOutput").ap()

    x_spill = nc.dram_tensor("x_spill", [128, DG * TC], F32).ap()
    xn_spill = nc.dram_tensor("xn_spill", [128, DG * TC], F32).ap()
    v_spill = nc.dram_tensor("v_spill", [128, DG * TC], BF16).ap()
    rows_d = nc.dram_tensor("rows_d", [1, 8 * 520], F32).ap()
    ccin_d = [nc.dram_tensor(f"ccin_d{i}", [128, 32], F32) for i in range(2)]
    ccout_d = [nc.dram_tensor(f"ccout_d{i}", [8 * 128, 32], F32)
               for i in range(2)]
    mid_spill = nc.dram_tensor("mid_spill", [128, DG * TC], F32).ap()
    res_spill = nc.dram_tensor("res_spill", [128, DG * TC], F32).ap()
    wgT_d = nc.dram_tensor("wgT_d", [8, 24, 128, 128], BF16).ap()
    wuT_d = nc.dram_tensor("wuT_d", [8, 24, 128, 128], BF16).ap()
    wdT_d = nc.dram_tensor("wdT_d", [24, 8, 128, 128], BF16).ap()

    dbg = {}
    if debug:
        for nm, shape, dt in [
            ("dbg_xm", [128, DG * TH], BF16),
            ("dbg_xf", [128, DG * TH], F32),
            ("dbg_xn", [128, DG * TH], F32),
            ("dbg_xc", [128, DG * TC], BF16),
            ("dbg_delta", [128, DG * TC], F32),
            ("dbg_bt", [16, 2 * TC], BF16),
            ("dbg_y", [128, DG * TC], F32),
            ("dbg_da", [128, TC], F32),
            ("dbg_ub", [128, TC], F32),
            ("dbg_h", [128, TC], F32),
            ("dbg_yacc1", [128, TC], F32),
            ("dbg_yacc2", [128, TC], F32),
            ("dbg_seeds", [128, NS], F32),
            ("dbg_gath", [128, 256], F32),
            ("dbg_mid", [128, DG * TC], F32),
            ("dbg_ao", [128, DG * TC], F32),
            ("dbg_x1res", [128, DG * TC], F32),
            ("dbg_ffn", [128, DG * TC], F32),
            ("dbg_g", [128, DG * TC], BF16),
            ("dbg_v", [128, DG * TC], BF16),
            ("dbg_vsp", [128, DG * TC], BF16),
        ]:
            dbg[nm] = nc.dram_tensor(nm, shape, dt, kind="ExternalOutput").ap()

    def sb(name, shape, dt):
        return es.enter_context(nc.sbuf_tensor(name, shape, dt))

    AR_A = sb("AR_A", [128, DG * TH], F32)
    AR_B = sb("AR_B", [128, 12288], BF16)
    AR_C = sb("AR_C", [128, DG * TH], BF16)
    AR_D = sb("AR_D", [128, DG * TH], BF16)
    WF = sb("WF", [128, 3072], F32)
    WB0 = sb("WB0", [128, 3072], BF16)
    WB1 = sb("WB1", [128, 1024], BF16)
    BtCt = sb("BtCt", [16, 2 * TC], BF16)
    RW = 520
    ROWS = sb("ROWS", [1, 4 * RW], F32)
    VLOAD = sb("VLOAD", [128, TC], BF16)
    SREP = sb("SREP", [128, 1024], F32)
    MREP = sb("MREP", [128, 1024], F32)
    SMALL = sb("SMALL", [128, SMALL_TOT], F32)
    BPCB = sb("BPCB", [16, 2], F32)
    IDF = sb("IDF", [128, 128], F32)
    IDB = sb("IDB", [128, 128], BF16)
    OOV = sb("OOV", [128, 1], BF16)
    GATH = sb("GATH", [128, 256], F32)
    INTS = WF[:, 0:1024].bitcast(I32)
    NSEL = sb("NSEL", [NS, NS * 128], BF16)

    def S(name, j0=0, w=1):
        o = SMAP[name]
        return SMALL[:, o + j0:o + j0 + w]

    PB = [es.enter_context(nc.psum_tensor(f"PB{i}", [128, 1024], F32))
          for i in range(4)]

    def pbank(i):
        return PB[i // 2][:, (i % 2) * 512:(i % 2) * 512 + 512]

    sD = Sem(nc, es, "sD")
    sG = Sem(nc, es, "sG")
    sVq = Sem(nc, es, "sVq")
    sSq = Sem(nc, es, "sSq")
    sP = Sem(nc, es, "sP")
    sV = Sem(nc, es, "sV")
    sS = Sem(nc, es, "sS")
    sX = Sem(nc, es, "sX")
    sCC = Sem(nc, es, "sCC")
    sRES = [Sem(nc, es, f"sRES{i}") for i in range(4)]
    sVSP = [Sem(nc, es, f"sVSP{i}") for i in range(4)]
    sVLD = Sem(nc, es, "sVLD")
    sROW = Sem(nc, es, "sROW")
    sBC0 = Sem(nc, es, "sBC0")
    sBC1 = Sem(nc, es, "sBC1")
    sSPL = [Sem(nc, es, f"sSPL{i}") for i in range(6)]
    sCC2 = Sem(nc, es, "sCC2")

    QSEM = {"sync": sD, "gpsimd": sG, "vector": sVq, "scalar": sSq}
    ESEM = {"vector": sV, "scalar": sS, "tensor": sP, "gpsimd": sX}

    def dma(queue, out, in_, sem=None, **kw):
        sem = QSEM[queue] if sem is None else sem
        tgt = sem.bump(16)
        def fn(e, out=out, in_=in_, kw=kw, sem=sem):
            e.dma_start(out=out, in_=in_, **kw).then_inc(sem.h, 16)
        sched.on(queue, fn)
        return tgt

    def wait(eng, sem, tgt):
        def fn(e, sem=sem, tgt=tgt):
            e.wait_ge(sem.h, tgt)
        sched.on(eng, fn)

    def op(eng, fn, inc=False):
        if inc:
            sem = ESEM[eng]
            tgt = sem.bump(1)
            def wrapped(e, fn=fn, sem=sem):
                fn(e).then_inc(sem.h, 1)
            sched.on(eng, wrapped)
            return tgt
        sched.on(eng, fn)
        return None

    def fm(arena, width, g, c0, c1):
        return arena[:, width * g + c0: width * g + c1]

    def row(r, cw):
        return ROWS[0:1, r * RW: r * RW + cw]

    def rowd(half, r, cw):
        return rows_d[0:1, (half * 4 + r) * 520: (half * 4 + r) * 520 + cw]

    dma("sync", SMALL[:, 0:NSMALL], smalls_in[:, :])
    dma("sync", IDF[:, :], ident_f[:, :])
    dma("sync", IDB[:, :], ident_b[:, :])
    dma("sync", OOV[:, :], oneover[:, :])
    dma("sync", BPCB[:, :], bpcb_in[:, :])
    dma("sync", NSEL[:, :], nsel_in[:, :])
    t_consts = sD.n

    wait("scalar", sD, t_consts)
    op("scalar", lambda e: e.activation(S("A", 0, DG * NS), S("A", 0, DG * NS),
                                        AF.Exp, scale=1.0))
    op("scalar", lambda e: e.mul(S("A", 0, DG * NS), S("A", 0, DG * NS), -1.0),
       inc=True)
    t_A = sS.n

    XTM = WF
    row_tiles = [(i * 128, min(TH, (i + 1) * 128)) for i in range((TH + 127) // 128)]
    tr_done = {}
    ev_done = {}
    for i, (r0, r1) in enumerate(row_tiles):
        buf = (i % 2) * 1024
        if i >= 2:
            wait("sync", sP, tr_done[i - 2])
        tL = dma("sync", XTM[0:r1 - r0, buf:buf + 1024], x_tm[r0:r1, :])
        wait("tensor", sD, tL)
        if i >= 1:
            wait("tensor", sV, ev_done[i - 1])
        for g in range(DG):
            def tr(e, r0=r0, r1=r1, g=g, buf=buf):
                pt, off = PB[g // 2], (g % 2) * 512
                return e.transpose(pt[:, off:off + (r1 - r0)],
                                   XTM[0:r1 - r0, buf + g * 128:buf + (g + 1) * 128],
                                   IDF[0:r1 - r0, 0:r1 - r0])
            tr_done[i] = op("tensor", tr, inc=(g == DG - 1))
        wait("vector", sP, tr_done[i])
        for g in range(DG):
            def ev(e, r0=r0, r1=r1, g=g):
                pt, off = PB[g // 2], (g % 2) * 512
                return e.tensor_copy(fm(AR_A, TH, g, r0, r1),
                                     pt[:, off:off + (r1 - r0)])
            ev_done[i] = op("vector", ev, inc=(g == DG - 1))
    t_xfm = ev_done[len(row_tiles) - 1]

    wait("scalar", sV, t_xfm)
    for g in range(DG):
        dma("scalar", x_spill[:, g * TC:(g + 1) * TC], fm(AR_A, TH, g, 1, 2049),
            sem=sSPL[0])
    t_xspill = sSPL[0].n
    if debug:
        for g in range(DG):
            dma("gpsimd", dbg["dbg_xf"][:, g * TH:(g + 1) * TH],
                fm(AR_A, TH, g, 0, TH))
        t_xspill = sG.n

    def emit_ln(src_arena, src_width, dst_write, tiles, gamma=None, beta=None,
                scale_extra=1.0, cast_buf=None, pre_waits=()):
        for eng, sem, tgt in pre_waits:
            wait(eng, sem, tgt)
        t_cast = None
        for g in range(DG):
            for (c0, c1) in tiles:
                def cst(e, g=g, c0=c0, c1=c1):
                    return e.tensor_copy(fm(cast_buf, src_width, g, c0, c1),
                                         fm(src_arena, src_width, g, c0, c1))
                r = op("vector", cst,
                       inc=(g == DG - 1 and (c0, c1) == tiles[-1]))
                t_cast = r if r is not None else t_cast
        SQ = WB1
        wait("scalar", sV, t_cast)
        wait("tensor", sV, t_cast)
        wait("scalar", sV, sV.n)
        last = None
        t_rowdma_prev = None
        for ti, (c0, c1) in enumerate(tiles):
            cw = c1 - c0
            half = ti % 2
            mmdone = {}
            for g in range(DG):
                sqoff = (g % 2) * 512
                if g >= 2:
                    wait("scalar", sP, mmdone[g - 2])
                def sq(e, g=g, c0=c0, c1=c1, sqoff=sqoff):
                    return e.activation(SQ[:, sqoff:sqoff + (c1 - c0)],
                                        fm(cast_buf, src_width, g, c0, c1),
                                        AF.Square)
                tS_ = op("scalar", sq, inc=True)
                wait("tensor", sS, tS_)
                def mm1(e, g=g, c0=c0, c1=c1, cw=cw):
                    return e.matmul(PB[2][0:1, 0:cw], OOV[:, 0:1],
                                    fm(cast_buf, src_width, g, c0, c1),
                                    start=(g == 0), stop=(g == DG - 1),
                                    skip_group_check=True)
                op("tensor", mm1)
                def mm2(e, g=g, cw=cw, sqoff=sqoff):
                    return e.matmul(PB[2][0:1, 512:512 + cw], OOV[:, 0:1],
                                    SQ[:, sqoff:sqoff + cw],
                                    start=(g == 0), stop=(g == DG - 1),
                                    skip_group_check=True)
                mmdone[g] = op("tensor", mm2, inc=True)
            tP_ = mmdone[DG - 1]
            wait("vector", sP, tP_)
            if t_rowdma_prev is not None:
                wait("vector", sROW, t_rowdma_prev)
            op("vector", lambda e, cw=cw: e.tensor_copy(row(0, cw),
                                                        PB[2][0:1, 0:cw]))
            tV_ = op("vector", lambda e, cw=cw: e.tensor_copy(
                row(1, cw), PB[2][0:1, 512:512 + cw]), inc=True)
            wait("tensor", sV, tV_)
            op("vector", lambda e, cw=cw: e.tensor_tensor(
                row(2, cw), row(0, cw), row(0, cw), ALU.mult))
            t_var = op("vector", lambda e, cw=cw: e.tensor_tensor(
                row(2, cw), row(1, cw), row(2, cw), ALU.subtract), inc=True)
            wait("scalar", sV, t_var)
            op("scalar", lambda e, cw=cw: e.activation(
                row(2, cw), row(2, cw), AF.Ln, bias=S("eps")[0:1, :]))
            t_rstd = op("scalar", lambda e, cw=cw: e.activation(
                row(2, cw), row(2, cw), AF.Exp, scale=-0.5), inc=True)
            wait("vector", sS, t_rstd)
            if scale_extra != 1.0:
                op("vector", lambda e, cw=cw: e.tensor_scalar_mul(
                    row(2, cw), row(2, cw), float(scale_extra)))
            t_rows = op("vector", lambda e, cw=cw: e.scalar_tensor_tensor(
                row(3, cw), row(0, cw), -1.0, row(2, cw),
                ALU.mult, ALU.mult), inc=True)
            wait("scalar", sV, t_rows)
            dma("scalar", rowd(half, 2, cw), row(2, cw), sem=sROW)
            t_rowdma_prev = dma("scalar", rowd(half, 3, cw), row(3, cw),
                                sem=sROW)
            wait("scalar", sROW, t_rowdma_prev)
            off = half * 512
            if last is not None:
                wait("scalar", sV, last)
            tb1 = dma("scalar", SREP[:, off:off + cw],
                      rowd(half, 2, cw).broadcast_to([128, cw]), sem=sBC0)
            tb2 = dma("scalar", MREP[:, off:off + cw],
                      rowd(half, 3, cw).broadcast_to([128, cw]), sem=sBC1)
            wait("vector", sBC0, tb1)
            wait("vector", sBC1, tb2)
            for g in range(DG):
                def ap1(e, g=g, c0=c0, c1=c1, off=off, cw=cw):
                    return e.tensor_tensor(fm(src_arena, src_width, g, c0, c1),
                                           fm(src_arena, src_width, g, c0, c1),
                                           SREP[:, off:off + cw], ALU.mult)
                op("vector", ap1)
                if gamma is None:
                    def ap2(e, g=g, c0=c0, c1=c1, off=off, cw=cw):
                        return e.tensor_tensor(dst_write(g, c0, c1),
                                               fm(src_arena, src_width, g, c0, c1),
                                               MREP[:, off:off + cw], ALU.add)
                    last = op("vector", ap2, inc=(g == DG - 1))
                else:
                    def ap2(e, g=g, c0=c0, c1=c1, off=off, cw=cw):
                        return e.tensor_tensor(fm(src_arena, src_width, g, c0, c1),
                                               fm(src_arena, src_width, g, c0, c1),
                                               MREP[:, off:off + cw], ALU.add)
                    op("vector", ap2)
                    def ap3(e, g=g, c0=c0, c1=c1):
                        return e.scalar_tensor_tensor(
                            dst_write(g, c0, c1),
                            fm(src_arena, src_width, g, c0, c1),
                            S(gamma, g, 1),
                            S(beta, g, 1).broadcast_to([128, c1 - c0]),
                            ALU.mult, ALU.add)
                    last = op("vector", ap3, inc=(g == DG - 1))
        return last

    t_ln0 = emit_ln(AR_A, TH, lambda g, c0, c1: fm(AR_A, TH, g, c0, c1),
                    TILES_TH, cast_buf=AR_C,
                    pre_waits=[("vector", sSPL[0], t_xspill),
                               ("vector", sD, t_consts)])
    wait("gpsimd", sV, t_ln0)
    if debug:
        for g in range(DG):
            dma("gpsimd", dbg["dbg_xn"][:, g * TH:(g + 1) * TH],
                fm(AR_A, TH, g, 0, TH))
    wait("scalar", sV, t_ln0)
    wait("scalar", sG, sG.n)
    for g in range(DG):
        dma("scalar", xn_spill[:, g * TC:(g + 1) * TC],
            fm(AR_A, TH, g, 1, 2049), sem=sSPL[1])
    t_xnspill = sSPL[1].n
    t_ln1 = emit_ln(AR_A, TH, lambda g, c0, c1: fm(AR_A, TH, g, c0, c1),
                    TILES_TH, gamma="ln1_g", beta="ln1_b", cast_buf=AR_C,
                    pre_waits=[("vector", sSPL[1], t_xnspill)])
    t_lnchain = emit_ln(AR_A, TH, lambda g, c0, c1: fm(AR_C, TH, g, c0, c1),
                        TILES_TH, gamma="ln_g", beta="ln_b", cast_buf=AR_D)

    if debug:
        wait("gpsimd", sV, t_lnchain)
        for g in range(DG):
            dma("gpsimd", dbg["dbg_xm"][:, g * TH:(g + 1) * TH],
                fm(AR_C, TH, g, 0, TH))

    def emit_finish():
        for eng, sem in [("gpsimd", sG), ("sync", sD), ("vector", sVq),
                         ("scalar", sSq)]:
            wait(eng, sem, sem.n)
        for sem in ([sVLD, sROW, sBC0, sBC1, sCC2] + sRES + sVSP + sSPL):
            wait("scalar", sem, sem.n)

    gemm_state = {"war": [], "pe": 0}

    def emit_gemm(wsrc_fn, nkc, mcols, src_fn, evac_fn, tiles,
                  pre_waits=(), first_bank=0, wdtype=F32):
        for eng, sem, tgt in pre_waits:
            wait(eng, sem, tgt)
        for w in gemm_state["war"]:
            wait("tensor", w[0], w[1])
        gemm_state["war"] = []
        wait("gpsimd", sP, gemm_state["pe"])
        wait("sync", sP, gemm_state["pe"])
        nmc = mcols // 128
        cast_t = {}
        pe_t = {}
        evac_war = []
        gi = 0
        for ti, (c0, c1) in enumerate(tiles):
            for w in evac_war:
                wait("tensor", w[0], w[1])
            evac_war = []
            for kc in range(nkc):
                wbuf = (gi % 2) * 1024
                if wdtype == BF16:
                    if gi >= 2:
                        wait("sync", sP, pe_t[gi - 2])
                    t_wf = dma("sync", WB0[:, wbuf:wbuf + mcols], wsrc_fn(kc))
                    wait("tensor", sD, t_wf)
                    for mci in range(nmc):
                        def mmb_(e, mci=mci, kc=kc, c0=c0, c1=c1, wbuf=wbuf):
                            ps = pbank(first_bank + mci)
                            return e.matmul(
                                ps[:, 0:c1 - c0],
                                WB0[:, wbuf + mci * 128:wbuf + (mci + 1) * 128],
                                src_fn(kc, c0, c1),
                                start=(kc == 0), stop=(kc == nkc - 1),
                                skip_group_check=True)
                        pe_t[gi] = op("tensor", mmb_, inc=(mci == nmc - 1))
                    gi += 1
                    continue
                if gi >= 2:
                    wait("sync", sX, cast_t[gi - 2])
                t_wf = dma("sync", WF[:, wbuf:wbuf + mcols], wsrc_fn(kc))
                wait("gpsimd", sD, t_wf)
                if gi >= 2:
                    wait("gpsimd", sP, pe_t[gi - 2])
                def cst(e, wbuf=wbuf, mcols=mcols):
                    return e.tensor_copy(WB0[:, wbuf:wbuf + mcols],
                                         WF[:, wbuf:wbuf + mcols])
                cast_t[gi] = op("gpsimd", cst, inc=True)
                wait("tensor", sX, cast_t[gi])
                for mci in range(nmc):
                    def mm(e, mci=mci, kc=kc, c0=c0, c1=c1, wbuf=wbuf):
                        ps = pbank(first_bank + mci)
                        return e.matmul(
                            ps[:, 0:c1 - c0],
                            WB0[:, wbuf + mci * 128:wbuf + (mci + 1) * 128],
                            src_fn(kc, c0, c1),
                            start=(kc == 0), stop=(kc == nkc - 1),
                            skip_group_check=True)
                    pe_t[gi] = op("tensor", mm, inc=(mci == nmc - 1))
                gi += 1
            tP_ = pe_t[gi - 1]
            for mci in range(nmc):
                evac_war.append(evac_fn(mci, c0, c1,
                                        pbank(first_bank + mci), tP_))
        gemm_state["war"] = list(evac_war)
        gemm_state["pe"] = pe_t[gi - 1]
        return evac_war

    def xm_src(kc, c0, c1):
        return fm(AR_C, TH, kc, c0 + 1, c1 + 1)

    def xm_src_th(kc, c0, c1):
        return fm(AR_C, TH, kc, c0, c1)

    def xc_src(kc, c0, c1):
        return fm(AR_D, TC, kc, c0, c1)

    if phases < 2:
        emit_finish()
        return finish(nc, es, sched)
    if debug:
        w1_pre = [("sync", sG, sG.n)]
    else:
        w1_pre = []
    vslot = {"i": 0, "dmas": {}}

    def ev_v1(mci, c0, c1, ps, tP_):
        i = vslot["i"]; vslot["i"] += 1
        off = (i % 4) * 512
        wait("scalar", sP, tP_)
        if i >= 4:
            wait("scalar", sVSP[i % 4], vslot["dmas"][i - 4])
        def f(e, mci=mci, c0=c0, c1=c1, ps=ps, off=off):
            return e.activation(VLOAD[:, off:off + (c1 - c0)],
                                ps[:, 0:c1 - c0], AF.Silu,
                                bias=S("v1_b", mci, 1))
        tS_ = op("scalar", f, inc=True)
        wait("scalar", sS, tS_)
        vslot["dmas"][i] = dma("scalar",
                               v_spill[:, mci * TC + c0:mci * TC + c1],
                               VLOAD[:, off:off + (c1 - c0)],
                               sem=sVSP[i % 4])
        return (sS, tS_)

    emit_gemm(lambda kc: W["v1_w"][kc * 128:(kc + 1) * 128, :],
              DG, 1024, xm_src, ev_v1, TILES_TC,
              pre_waits=[("tensor", sV, t_lnchain),
                         ("sync", sP, sP.n)] + w1_pre)
    t_vspill = sG.n

    if phases < 3:
        emit_finish()
        return finish(nc, es, sched)
    def ev_w1(mci, c0, c1, ps, tP_):
        wait("scalar", sP, tP_)
        def f(e, mci=mci, c0=c0, c1=c1, ps=ps):
            return e.activation(fm(AR_C, TH, mci, c0, c1),
                                ps[:, 0:c1 - c0], AF.Identity,
                                bias=S("w1_b", mci, 1))
        return (sS, op("scalar", f, inc=True))

    emit_gemm(lambda kc: W["w1_w"][kc * 128:(kc + 1) * 128, :],
              DG, 1024, xm_src_th, ev_w1, TILES_TH,
              pre_waits=[("scalar", sG, sG.n)])
    t_x1 = sS.n

    def AC(i, c0=0, c1=TC):
        return AR_A[:, i * TC + c0:i * TC + c1]

    wait("vector", sS, t_x1)
    for g in range(DG):
        def mskl(e, g=g):
            return e.tensor_tensor(fm(AR_C, TH, g, 0, 1),
                                   fm(AR_C, TH, g, 0, 1),
                                   S("mask", 0, 1), ALU.mult)
        op("vector", mskl)
        def mskr(e, g=g):
            return e.tensor_tensor(fm(AR_C, TH, g, 2049, 2050),
                                   fm(AR_C, TH, g, 2049, 2050),
                                   S("mask", 1, 1), ALU.mult)
        op("vector", mskr)
    t_xc = None
    for g in range(DG):
        def cv0(e, g=g):
            return e.tensor_scalar_mul(AC(4), fm(AR_C, TH, g, 0, 2048),
                                       S("conv", 0 * DG + g, 1))
        op("vector", cv0)
        def cv1(e, g=g):
            return e.scalar_tensor_tensor(AC(4), fm(AR_C, TH, g, 1, 2049),
                                          S("conv", 1 * DG + g, 1), AC(4),
                                          ALU.mult, ALU.add)
        op("vector", cv1)
        def cv2(e, g=g):
            return e.scalar_tensor_tensor(AC(4), fm(AR_C, TH, g, 2, 2050),
                                          S("conv", 2 * DG + g, 1), AC(4),
                                          ALU.mult, ALU.add)
        t_cv = op("vector", cv2, inc=True)
        wait("scalar", sV, t_cv)
        def cvs(e, g=g):
            return e.activation(fm(AR_D, TC, g, 0, TC), AC(4), AF.Silu)
        t_xc = op("scalar", cvs, inc=True)
        wait("vector", sS, t_xc)

    if debug:
        wait("gpsimd", sS, t_xc)
        for g in range(DG):
            dma("gpsimd", dbg["dbg_xc"][:, g * TC:(g + 1) * TC],
                fm(AR_D, TC, g, 0, TC))

    if phases < 4:
        emit_finish()
        return finish(nc, es, sched)
    t_bpw = dma("sync", WF[:, 0:128].rearrange("p (k n) -> p k n", k=DG),
                W["Bp_w"].rearrange("(k p) n -> p k n", p=128))
    t_cpw = dma("sync", WF[:, 128:256].rearrange("p (k n) -> p k n", k=DG),
                W["Cp_w"].rearrange("(k p) n -> p k n", p=128))
    wait("gpsimd", sD, t_cpw)
    wait("gpsimd", sP, gemm_state["pe"])
    t_bpc = op("gpsimd", lambda e: e.tensor_copy(WB0[:, 0:256], WF[:, 0:256]),
               inc=True)
    wait("tensor", sX, t_bpc)
    wait("tensor", sS, t_xc)
    prev_ev = None
    for w in gemm_state["war"]:
        wait("tensor", w[0], w[1])
    gemm_state["war"] = []
    for which in (0, 1):
        for ti, (c0, c1) in enumerate(TILES_TC):
            if prev_ev is not None:
                wait("tensor", sS, prev_ev)
            for kc in range(DG):
                def mmb(e, which=which, kc=kc, c0=c0, c1=c1):
                    return e.matmul(
                        pbank(4)[0:NS, 0:c1 - c0],
                        WB0[:, which * 128 + kc * NS:which * 128 + (kc + 1) * NS],
                        xc_src(kc, c0, c1),
                        start=(kc == 0), stop=(kc == DG - 1),
                        skip_group_check=True)
                t_mmb = op("tensor", mmb, inc=(kc == DG - 1))
            wait("scalar", sP, t_mmb)
            def evb(e, which=which, c0=c0, c1=c1):
                return e.activation(
                    BtCt[0:NS, which * TC + c0:which * TC + c1],
                    pbank(4)[0:NS, 0:c1 - c0], AF.Identity,
                    bias=BPCB[0:NS, which:which + 1])
            prev_ev = op("scalar", evb, inc=True)
    t_btct = prev_ev
    gemm_state["pe"] = sP.n

    if debug:
        wait("gpsimd", sS, t_btct)
        dma("gpsimd", dbg["dbg_bt"][:, :], BtCt[0:NS, :])

    if phases < 5:
        emit_finish()
        return finish(nc, es, sched)
    ones_b = S("onecol", 0, 1).broadcast_to([128, TC])
    scan_bank_war = None
    t_gate_done = None
    for g in range(DG):
        if g == 0:
            for ss in sVSP:
                wait("scalar", ss, ss.n)
        elif t_gate_done is not None:
            wait("scalar", sV, t_gate_done)
        t_vload = dma("scalar", VLOAD[:, 0:TC],
                      v_spill[:, g * TC:(g + 1) * TC], sem=sVLD)

        def ev_dt(mci, c0, c1, ps, tP_, g=g):
            wait("scalar", sP, tP_)
            def f(e, c0=c0, c1=c1, ps=ps, g=g):
                return e.activation(AC(0, c0, c1), ps[:, 0:c1 - c0], AF.Identity,
                                    bias=S("dt_b", g, 1))
            return (sS, op("scalar", f, inc=True))
        pre = [("tensor", sS, t_btct)]
        if scan_bank_war is not None:
            pre.append(("tensor", sV, scan_bank_war))
        war = emit_gemm(
            lambda kc, g=g: W["dt_w"][kc * 128:(kc + 1) * 128,
                                      g * 128:(g + 1) * 128],
            DG, 128, xc_src, ev_dt, TILES_TC, pre_waits=pre, first_bank=0)
        for ssem, tgt in war:
            wait("scalar", ssem, tgt)
            wait("tensor", ssem, tgt)
        op("scalar", lambda e: e.activation(AC(4), AC(0), AF.Abs))
        op("scalar", lambda e: e.activation(AC(4), AC(4), AF.Exp, scale=-1.0))
        op("scalar", lambda e: e.activation(AC(4), AC(4), AF.Ln,
                                            bias=S("onecol")[0:128, 0:1]))
        t_sp = op("scalar", lambda e: e.activation(AC(7), AC(0), AF.Relu),
                  inc=True)
        wait("vector", sS, t_sp)
        t_dlt = op("vector", lambda e: e.tensor_tensor(AC(0), AC(7), AC(4),
                                                       ALU.add), inc=True)
        if debug:
            wait("gpsimd", sV, t_dlt)
            t_dbgd = dma("gpsimd", dbg["dbg_delta"][:, g * TC:(g + 1) * TC],
                         AC(0))
            wait("scalar", sG, t_dbgd)
        t_sdl = op("vector", lambda e: e.tensor_tensor_scan(
            AC(1), ones_b, AC(0), 0.0, ALU.mult, ALU.add), inc=True)
        wait("gpsimd", sV, t_dlt)
        t_u = op("gpsimd", lambda e, g=g: e.tensor_tensor(
            AC(2), AC(0), fm(AR_D, TC, g, 0, TC), ALU.mult), inc=True)
        wait("vector", sX, t_u)

        if SCANPART < 2:
            continue
        t_tts_prev = None
        for n in range(NS):
            if scan_bank_war is not None:
                wait("tensor", sV, scan_bank_war)
            for which in (0, 1):
                for ti, (c0, c1) in enumerate(TILES_TC):
                    def bld(e, n=n, which=which, ti=ti, c0=c0, c1=c1):
                        ps = pbank(which * 4 + ti)
                        return e.matmul(
                            ps[:, 0:c1 - c0],
                            NSEL[:, n * 128:(n + 1) * 128],
                            BtCt[0:NS, which * TC + c0:which * TC + c1],
                            start=True, stop=True, skip_group_check=True)
                    t_bld = op("tensor", bld,
                               inc=(which == 1 and ti == 3))
            if n == 0:
                wait("scalar", sV, t_dlt)
            else:
                wait("scalar", sV, t_tts_prev)
            def fda(e, g=g, n=n):
                return e.activation(AC(4), AC(0), AF.Exp,
                                    scale=S("A", g * NS + n, 1))
            t_da = op("scalar", fda, inc=True)
            wait("vector", sP, t_bld)
            wait("vector", sS, t_da)
            op("vector", lambda e: e.tensor_tensor(
                AC(5, 0, 1024), AC(2, 0, 1024), PB[0][:, :], ALU.mult))
            op("vector", lambda e: e.tensor_tensor(
                AC(5, 1024, 2048), AC(2, 1024, 2048), PB[1][:, :], ALU.mult))
            t_tts_prev = op("vector", lambda e: e.tensor_tensor_scan(
                AC(6), AC(4), AC(5), 0.0, ALU.mult, ALU.add), inc=True)
            wait("vector", sV, t_tts_prev)
            def cci(e, n=n):
                return e.tensor_copy(S("ccin", n, 1), AC(6, 2047, 2048))
            op("vector", cci)
            op("vector", lambda e: e.tensor_tensor(
                AC(7, 0, 1024), AC(6, 0, 1024), PB[2][:, :], ALU.mult))
            scan_bank_war = op("vector", lambda e: e.tensor_tensor(
                AC(7, 1024, 2048), AC(6, 1024, 2048), PB[3][:, :], ALU.mult),
                inc=True)
            if debug and g == 0 and n == 0:
                wait("gpsimd", sV, scan_bank_war)
                dma("gpsimd", dbg["dbg_da"][:, :], AC(4))
                dma("gpsimd", dbg["dbg_ub"][:, :], AC(5))
                t_dbg0 = dma("gpsimd", dbg["dbg_h"][:, :], AC(6))
                wait("vector", sG, t_dbg0)
                wait("scalar", sG, t_dbg0)
            if n == 0:
                op("vector", lambda e: e.tensor_copy(AC(3), AC(7)))
            else:
                op("vector", lambda e: e.tensor_tensor(AC(3), AC(3), AC(7),
                                                       ALU.add))

        if debug and g == 0:
            wait("gpsimd", sV, sV.n)
            t_dy1 = dma("gpsimd", dbg["dbg_yacc1"][:, :], AC(3))
            wait("vector", sG, t_dy1)
        if SCANPART < 3:
            continue
        wait("scalar", sV, t_sdl)
        t_dtot = None
        for n in range(NS):
            def fdt(e, g=g, n=n):
                return e.activation(S("ccin", NS + n, 1), AC(1, 2047, 2048),
                                    AF.Exp, scale=S("A", g * NS + n, 1))
            t_dtot = op("scalar", fdt, inc=(n == NS - 1))
        wait("scalar", sV, scan_bank_war)
        wait("scalar", sS, t_dtot)
        t_cc1 = dma("scalar", ccin_d[g % 2].ap(), S("ccin", 0, 32), sem=sCC2)
        wait("gpsimd", sCC2, t_cc1)
        cct = sCC.bump(1)
        def ccw(e, g=g):
            e.collective_compute(
                "AllGather", ALU.bypass, replica_groups=[list(range(NCORES))],
                ins=[ccin_d[g % 2].ap().opt()],
                outs=[ccout_d[g % 2].ap().opt()]).then_inc(sCC.h, 1)
        sched.on("gpsimd", ccw)
        wait("scalar", sCC, cct)
        t_gth = dma("scalar", GATH[:, :].rearrange("p (r c) -> p r c", r=8),
                    ccout_d[g % 2].ap().rearrange("(r p) c -> p r c", p=128),
                    sem=sCC2)
        wait("vector", sCC2, t_gth)
        t_c = op("vector", lambda e: e.memset(S("Rbuf", 0, NS), 0.0), inc=True)
        wait("vector", sV, t_c)
        t_c = op("vector", lambda e: e.memset(S("seeds", 0, NS), 0.0), inc=True)
        wait("vector", sV, t_c)
        for j in range(7):
            def cj1(e, j=j):
                return e.tensor_tensor(S("Rbuf", 0, NS), S("Rbuf", 0, NS),
                                       GATH[:, j * 32 + NS:j * 32 + 2 * NS],
                                       ALU.mult)
            t_c = op("vector", cj1, inc=True)
            wait("vector", sV, t_c)
            def cj2(e, j=j):
                return e.tensor_tensor(S("Rbuf", 0, NS), S("Rbuf", 0, NS),
                                       GATH[:, j * 32:j * 32 + NS], ALU.add)
            t_c = op("vector", cj2, inc=True)
            wait("vector", sV, t_c)
            def cj3(e, j=j):
                return e.scalar_tensor_tensor(S("seeds", 0, NS),
                                              S("Rbuf", 0, NS),
                                              S("sel", j, 1),
                                              S("seeds", 0, NS),
                                              ALU.mult, ALU.add)
            t_c = op("vector", cj3, inc=True)
            wait("vector", sV, t_c)

        if debug and g == 0:
            wait("gpsimd", sV, sV.n)
            dma("gpsimd", dbg["dbg_seeds"][:, :], S("seeds", 0, NS))
            t_dsg = dma("gpsimd", dbg["dbg_gath"][:, :], GATH[:, :])
            wait("vector", sG, t_dsg)
        if SCANPART < 4:
            continue
        for n in range(NS):
            wait("tensor", sV, scan_bank_war)
            for ti, (c0, c1) in enumerate(TILES_TC):
                def bld2(e, n=n, ti=ti, c0=c0, c1=c1):
                    ps = pbank(4 + ti)
                    return e.matmul(
                        ps[:, 0:c1 - c0],
                        NSEL[:, n * 128:(n + 1) * 128],
                        BtCt[0:NS, TC + c0:TC + c1],
                        start=True, stop=True, skip_group_check=True)
                t_bld2 = op("tensor", bld2, inc=(ti == 3))
            wait("scalar", sV, scan_bank_war)
            def fD(e, g=g, n=n):
                return e.activation(AC(4), AC(1), AF.Exp,
                                    scale=S("A", g * NS + n, 1))
            t_D = op("scalar", fD, inc=True)
            wait("vector", sP, t_bld2)
            wait("vector", sS, t_D)
            def yc1(e, n=n):
                return e.scalar_tensor_tensor(AC(7, 0, 1024), AC(4, 0, 1024),
                                              S("seeds", n, 1), PB[2][:, :],
                                              ALU.mult, ALU.mult)
            op("vector", yc1)
            def yc2(e, n=n):
                return e.scalar_tensor_tensor(AC(7, 1024, 2048),
                                              AC(4, 1024, 2048),
                                              S("seeds", n, 1), PB[3][:, :],
                                              ALU.mult, ALU.mult)
            scan_bank_war = op("vector", yc2, inc=True)
            op("vector", lambda e: e.tensor_tensor(AC(3), AC(3), AC(7),
                                                   ALU.add))

        if debug and g == 0:
            wait("gpsimd", sV, sV.n)
            t_dy2 = dma("gpsimd", dbg["dbg_yacc2"][:, :], AC(3))
            wait("vector", sG, t_dy2)
        if SCANPART < 5:
            continue
        def fssm(e, g=g):
            return e.scalar_tensor_tensor(AC(6), fm(AR_D, TC, g, 0, TC),
                                          S("D_p", g, 1), AC(3),
                                          ALU.mult, ALU.add)
        op("vector", fssm)
        wait("vector", sVLD, t_vload)
        def fgate(e, g=g):
            return e.tensor_tensor(fm(AR_C, TH, g, 0, TC), AC(6),
                                   VLOAD[:, 0:TC], ALU.mult)
        t_gate_done = op("vector", fgate, inc=True)
        if debug:
            wait("gpsimd", sV, t_gate_done)
            dma("gpsimd", dbg["dbg_g"][:, g * TC:(g + 1) * TC],
                fm(AR_C, TH, g, 0, TC))
            t_dv = dma("gpsimd", dbg["dbg_v"][:, g * TC:(g + 1) * TC],
                       VLOAD[:, 0:TC])
            wait("vector", sG, t_dv)
        if debug:
            wait("gpsimd", sV, t_gate_done)
            t_dbgy = dma("gpsimd", dbg["dbg_y"][:, g * TC:(g + 1) * TC], AC(6))
            wait("vector", sG, t_dbgy)

    if phases >= 6:
        rs = {"i": 0, "dmas": {}}

        def resbuf(i):
            t = SREP if (i % 4) < 2 else MREP
            off = (i % 2) * 512
            return t[:, off:off + 512]

        def ev_w2(mci, c0, c1, ps, tP_):
            i = rs["i"]; rs["i"] += 1
            if i >= 4:
                wait("scalar", sV, rs["dmas"][i - 4][1])
            t_r = dma("scalar", resbuf(i)[:, 0:c1 - c0],
                      xn_spill[:, mci * TC + c0:mci * TC + c1],
                      sem=sRES[i % 4])
            wait("vector", sP, tP_)
            wait("vector", sRES[i % 4], t_r)
            def f(e, mci=mci, c0=c0, c1=c1, ps=ps, i=i):
                return e.scalar_tensor_tensor(AC(mci, c0, c1),
                                              ps[:, 0:c1 - c0],
                                              S("w2_b", mci, 1),
                                              resbuf(i)[:, 0:c1 - c0],
                                              ALU.add, ALU.add)
            tv = op("vector", f, inc=True)
            rs["dmas"][i] = (sV, tv)
            return (sV, tv)

        emit_gemm(lambda kc: W["w2_w"][kc * 128:(kc + 1) * 128, :],
                  DG, 1024, lambda kc, c0, c1: fm(AR_C, TH, kc, c0, c1),
                  ev_w2, TILES_TC,
                  pre_waits=[("tensor", sV, t_gate_done),
                             ("gpsimd", sV, t_gate_done)])
        t_mid = sV.n
        if debug:
            wait("gpsimd", sV, t_mid)
            for g in range(DG):
                dma("gpsimd", dbg["dbg_mid"][:, g * TC:(g + 1) * TC], AC(g))
        wait("scalar", sV, t_mid)
        wait("scalar", sG, sG.n)
        for g in range(DG):
            dma("scalar", mid_spill[:, g * TC:(g + 1) * TC], AC(g),
                sem=sSPL[2])
        t_midspill = sSPL[2].n

        if PH6PART < 2:
            emit_finish()
            return finish(nc, es, sched)
        t_xn2 = emit_ln(AR_A, TC, lambda g, c0, c1: fm(AR_D, TC, g, c0, c1),
                        TILES_TC, gamma="ln2_g", beta="ln2_b", cast_buf=AR_C,
                        pre_waits=[("vector", sSPL[2], t_midspill)])

        if PH6PART < 3:
            emit_finish()
            return finish(nc, es, sched)
        rs2 = {"i": 0, "dmas": {}}
        ao_done = None
        for ti, (c0, c1) in enumerate(TILES_TC):
            for mp in range(3):
                def ev_ff1(mci, cc0, cc1, ps, tP_, mp=mp, c0=c0):
                    wait("scalar", sP, tP_)
                    mabs = mp * 8 + mci
                    def f(e, mci=mci, cc0=cc0, cc1=cc1, ps=ps, mabs=mabs,
                          c0=c0):
                        return e.activation(
                            AR_B[:, mabs * 512 + (cc0 - c0):
                                 mabs * 512 + (cc1 - c0)],
                            ps[:, 0:cc1 - cc0], AF.Gelu,
                            bias=S("ff1_b", mabs, 1))
                    return (sS, op("scalar", f, inc=True))
                pre = [("tensor", sV, t_xn2)] if (ti == 0 and mp == 0) else []
                if ao_done is not None and mp == 0:
                    pre.append(("tensor", ao_done[0], ao_done[1]))
                emit_gemm(lambda kc, mp=mp: W["ff1_w"][kc * 128:(kc + 1) * 128,
                                                       mp * 1024:(mp + 1) * 1024],
                          DG, 1024,
                          lambda kc, cc0, cc1: fm(AR_D, TC, kc, cc0, cc1),
                          ev_ff1, [(c0, c1)], pre_waits=pre)
            def ev_ff2(mci, cc0, cc1, ps, tP_, c0=c0):
                i = rs2["i"]; rs2["i"] += 1
                if i >= 4:
                    wait("scalar", sV, rs2["dmas"][i - 4][1])
                t_r = dma("scalar", resbuf(i)[:, 0:cc1 - cc0],
                          mid_spill[:, mci * TC + cc0:mci * TC + cc1],
                          sem=sRES[i % 4])
                wait("vector", sP, tP_)
                wait("vector", sRES[i % 4], t_r)
                def f(e, mci=mci, cc0=cc0, cc1=cc1, ps=ps, i=i):
                    return e.scalar_tensor_tensor(AC(mci, cc0, cc1),
                                                  ps[:, 0:cc1 - cc0],
                                                  S("ff2_b", mci, 1),
                                                  resbuf(i)[:, 0:cc1 - cc0],
                                                  ALU.add, ALU.add)
                tv = op("vector", f, inc=True)
                rs2["dmas"][i] = (sV, tv)
                return (sV, tv)
            war2 = emit_gemm(
                lambda kc, c0=c0: W["ff2_w"][kc * 128:(kc + 1) * 128, :],
                24, 1024,
                lambda kc, cc0, cc1, c0=c0: AR_B[:, kc * 512 + (cc0 - c0):
                                                 kc * 512 + (cc1 - c0)],
                ev_ff2, [(c0, c1)],
                pre_waits=[("tensor", sS, sS.n)])
            ao_done = war2[-1]
        t_ao = sV.n
        if debug:
            wait("gpsimd", sV, t_ao)
            for g in range(DG):
                dma("gpsimd", dbg["dbg_ao"][:, g * TC:(g + 1) * TC], AC(g))

        if PH6PART < 4:
            emit_finish()
            return finish(nc, es, sched)
        t_lno1 = emit_ln(AR_A, TC, lambda g, c0, c1: fm(AR_A, TC, g, c0, c1),
                         TILES_TC, cast_buf=AR_C,
                         pre_waits=[("vector", sG, sG.n)])
        wait("scalar", sV, t_lno1)
        rs3 = {"i": 0}
        t_x1res = None
        for g in range(DG):
            for (c0, c1) in TILES_TC:
                i = rs3["i"]; rs3["i"] += 1
                if t_x1res is not None:
                    wait("scalar", sV, t_x1res)
                t_r = dma("scalar", resbuf(i)[:, 0:c1 - c0],
                          x_spill[:, g * TC + c0:g * TC + c1],
                          sem=sRES[i % 4])
                wait("vector", sRES[i % 4], t_r)
                def fadd(e, g=g, c0=c0, c1=c1, i=i):
                    return e.tensor_tensor(AC(g, c0, c1), AC(g, c0, c1),
                                           resbuf(i)[:, 0:c1 - c0], ALU.add)
                t_x1res = op("vector", fadd, inc=True)
        if debug:
            wait("gpsimd", sV, t_x1res)
            for g in range(DG):
                dma("gpsimd", dbg["dbg_x1res"][:, g * TC:(g + 1) * TC], AC(g))
        wait("scalar", sV, t_x1res)
        wait("scalar", sG, sG.n)
        for g in range(DG):
            dma("scalar", res_spill[:, g * TC:(g + 1) * TC], AC(g),
                sem=sSPL[3])
        t_resspill = sSPL[3].n

    if phases >= 7:
        tposed = []
        wait("tensor", sV, sV.n)
        wait("vector", sP, sP.n)
        wait("gpsimd", sP, sP.n)
        wait("sync", sP, sP.n)
        for w in gemm_state["war"]:
            wait("tensor", w[0], w[1])
        gemm_state["war"] = []
        stg_t = None
        for mat, dst, nrt, ncsub in [(W["wg"], wgT_d, 24, 8),
                                     (W["wu"], wuT_d, 24, 8),
                                     (W["wd"], wdT_d, 8, 24)]:
            for rt in range(nrt):
                nsub = ncsub // 8
                for sub in range(nsub):
                    if stg_t is not None:
                        wait("sync", sV, stg_t[0])
                    t_i = dma("sync", INTS,
                              mat[rt * 128:(rt + 1) * 128,
                                  sub * 1024:(sub + 1) * 1024])
                    wait("vector", sD, t_i)
                    if stg_t is not None:
                        wait("vector", sP, stg_t[1])
                    t_cast = op("vector", lambda e: e.tensor_copy(
                        WB0[:, 0:1024], INTS), inc=True)
                    wait("tensor", sV, t_cast)
                    for j in range(8):
                        def ftr(e, j=j):
                            return e.transpose(
                                PB[3].bitcast(BF16)[:, j * 128:(j + 1) * 128],
                                WB0[:, j * 128:(j + 1) * 128], IDB[:, :])
                        t_tp = op("tensor", ftr, inc=(j == 7))
                    wait("vector", sP, t_tp)
                    if stg_t is not None:
                        wait("vector", sSPL[4], stg_t[2])
                    for j in range(8):
                        def fev(e, j=j):
                            return e.tensor_copy(
                                VLOAD[:, j * 128:(j + 1) * 128],
                                PB[3].bitcast(BF16)[:, j * 128:(j + 1) * 128])
                        t_ev = op("vector", fev, inc=(j == 7))
                    wait("scalar", sV, t_ev)
                    kbase = sub * 8
                    t_out = dma("scalar",
                                dst[kbase:kbase + 8, rt].rearrange(
                                    "k p c -> p k c"),
                                VLOAD[:, 0:1024].rearrange(
                                    "p (k c) -> p k c", k=8), sem=sSPL[4])
                    stg_t = (t_cast, t_tp, t_out)
        t_prep = stg_t
        wait("sync", sSPL[4], t_prep[2])
        gemm_state["pe"] = sP.n

    if phases >= 8:
        t_xg = emit_ln(AR_A, TC, lambda g, c0, c1: fm(AR_D, TC, g, c0, c1),
                       TILES_TC, cast_buf=AR_C, scale_extra=1.0 / 127.0,
                       pre_waits=[("vector", sSPL[3], t_resspill)])

        gl_done = None
        for ti, (c0, c1) in enumerate(TILES_TC):
            for dstbuf, wtd in ((AR_C, wgT_d), (AR_B, wuT_d)):
                for mp in range(3):
                    def ev_g(mci, cc0, cc1, ps, tP_, mp=mp, c0=c0,
                             dstbuf=dstbuf):
                        wait("scalar", sP, tP_)
                        mabs = mp * 8 + mci
                        fn = AF.Silu if dstbuf is AR_C else AF.Identity
                        def f(e, mci=mci, cc0=cc0, cc1=cc1, ps=ps,
                              mabs=mabs, c0=c0, fn=fn):
                            return e.activation(
                                dstbuf[:, mabs * 512 + (cc0 - c0):
                                       mabs * 512 + (cc1 - c0)],
                                ps[:, 0:cc1 - cc0], fn)
                        return (sS, op("scalar", f, inc=True))
                    pre = []
                    if ti == 0 and mp == 0 and dstbuf is AR_C:
                        pre = [("tensor", sV, t_xg), ("sync", sG, sG.n)]
                    if gl_done is not None and mp == 0 and dstbuf is AR_C:
                        pre.append(("tensor", gl_done[0], gl_done[1]))
                    def wsrc(kc, wtd=wtd, mp=mp):
                        return wtd[kc, mp * 8:(mp + 1) * 8].rearrange(
                            "m p c -> p m c")
                    emit_gemm(wsrc, DG, 1024,
                              lambda kc, cc0, cc1: fm(AR_D, TC, kc, cc0, cc1),
                              ev_g, [(c0, c1)], pre_waits=pre, wdtype=BF16)
            wait("vector", sS, sS.n)
            t_gu = op("vector", lambda e: e.tensor_tensor(
                AR_B[:, 0:12288], AR_B[:, 0:12288], AR_C[:, 0:12288],
                ALU.mult), inc=True)
            wait("vector", sV, t_gu)
            def ev_dn(mci, cc0, cc1, ps, tP_, c0=c0):
                wait("vector", sP, tP_)
                def f(e, mci=mci, cc0=cc0, cc1=cc1, ps=ps):
                    return e.tensor_scalar_mul(AC(mci, cc0, cc1),
                                               ps[:, 0:cc1 - cc0],
                                               1.0 / 127.0)
                return (sV, op("vector", f, inc=True))
            def wsrc_d(kc):
                return wdT_d[kc, 0:8].rearrange("m p c -> p m c")
            war3 = emit_gemm(
                wsrc_d, 24, 1024,
                lambda kc, cc0, cc1, c0=c0: AR_B[:, kc * 512 + (cc0 - c0):
                                                 kc * 512 + (cc1 - c0)],
                ev_dn, [(c0, c1)],
                pre_waits=[("tensor", sV, t_gu)], wdtype=BF16)
            gl_done = war3[-1]
        t_ffn = sV.n
        if debug:
            wait("gpsimd", sV, t_ffn)
            for g in range(DG):
                dma("gpsimd", dbg["dbg_ffn"][:, g * TC:(g + 1) * TC], AC(g))

        t_lno3 = emit_ln(AR_A, TC, lambda g, c0, c1: fm(AR_A, TC, g, c0, c1),
                         TILES_TC, cast_buf=AR_C,
                         pre_waits=[("vector", sG, sG.n)])
        wait("scalar", sV, t_lno3)
        rs4 = {"i": 0}
        t_fin = None
        for g in range(DG):
            for (c0, c1) in TILES_TC:
                i = rs4["i"]; rs4["i"] += 1
                if t_fin is not None:
                    wait("scalar", sV, t_fin)
                t_r = dma("scalar", resbuf(i)[:, 0:c1 - c0],
                          res_spill[:, g * TC + c0:g * TC + c1],
                          sem=sRES[i % 4])
                wait("vector", sRES[i % 4], t_r)
                def fadd2(e, g=g, c0=c0, c1=c1, i=i):
                    return e.tensor_tensor(AC(g, c0, c1), AC(g, c0, c1),
                                           resbuf(i)[:, 0:c1 - c0], ALU.add)
                t_fin = op("vector", fadd2, inc=True)
        wait("tensor", sV, t_fin)
        prev_out = None
        for rt in range(16):
            for gg in range(DG):
                def ftro(e, rt=rt, gg=gg):
                    pt, off = PB[gg // 2], (gg % 2) * 512
                    return e.transpose(pt[:, off:off + 128],
                                       AC(gg, rt * 128, (rt + 1) * 128),
                                       IDF[:, :])
                t_tro = op("tensor", ftro, inc=(gg == DG - 1))
            wait("vector", sP, t_tro)
            if prev_out is not None:
                wait("vector", sD, prev_out)
            for gg in range(DG):
                def fevo(e, gg=gg):
                    pt, off = PB[gg // 2], (gg % 2) * 512
                    return e.tensor_copy(WF[:, gg * 128:(gg + 1) * 128],
                                         pt[:, off:off + 128])
                t_evo = op("vector", fevo, inc=(gg == DG - 1))
            wait("tensor", sV, t_evo)
            wait("sync", sV, t_evo)
            prev_out = dma("sync", y_out[rt * 128:(rt + 1) * 128, :],
                           WF[:, 0:1024])

    emit_finish()
    return finish(nc, es, sched)


def finish(nc, es, sched):
    with nc.Block() as block:
        @block.sync
        def _(e):
            for fn in sched.q["sync"]:
                fn(e)

        @block.vector
        def _(e):
            for fn in sched.q["vector"]:
                fn(e)

        @block.scalar
        def _(e):
            for fn in sched.q["scalar"]:
                fn(e)

        @block.tensor
        def _(e):
            for fn in sched.q["tensor"]:
                fn(e)

        @block.gpsimd
        def _(e):
            for fn in sched.q["gpsimd"]:
                fn(e)
    es.close()
    return nc


def _cols(v):
    return np.asarray(v, np.float32).reshape(-1, 128).T


def _prep_inputs(inputs):
    import ml_dtypes
    x = np.asarray(inputs["L0_attn_x"])[0]
    xpad = np.zeros((16384 + 2, D), np.float32)
    xpad[1:-1] = x
    ident = np.eye(128)
    g = lambda nm: np.asarray(inputs[nm])

    conv = g("L0_attn_conv_w")[:, 0, :]
    conv_cols = np.concatenate([_cols(conv[:, kk]) for kk in range(3)], axis=1)
    alog = g("L0_attn_A_log")
    a_cols = alog.reshape(DG, 128, NS).transpose(1, 0, 2).reshape(128, DG * NS)
    base_cols = [
        _cols(g("L0_attn_w1_b")), _cols(g("L0_attn_v1_b")),
        _cols(g("L0_attn_w2_b")), _cols(g("L0_attn_dt_b")),
        _cols(g("L0_attn_ff2_b")),
        np.asarray(g("L0_attn_ff1_b"), np.float32).reshape(-1, 128).T,
        _cols(g("L0_attn_ln1_g")), _cols(g("L0_attn_ln1_b")),
        _cols(g("L0_attn_ln_g")), _cols(g("L0_attn_ln_b")),
        _cols(g("L0_attn_ln2_g")), _cols(g("L0_attn_ln2_b")),
        conv_cols, _cols(g("L0_attn_D_p")), a_cols,
    ]
    bpcb = np.stack([np.asarray(g("L0_attn_Bp_b"), np.float32),
                     np.asarray(g("L0_attn_Cp_b"), np.float32)], axis=1)
    in_maps = []
    for k in range(NCORES):
        m = {}
        m["x_tm"] = np.ascontiguousarray(xpad[k * TC:k * TC + TH])
        for src, dst in [
            ("L0_attn_w1_w", "w1_w"), ("L0_attn_v1_w", "v1_w"),
            ("L0_attn_w2_w", "w2_w"), ("L0_attn_dt_w", "dt_w"),
            ("L0_attn_Bp_w", "Bp_w"), ("L0_attn_Cp_w", "Cp_w"),
            ("L0_attn_ff1_w", "ff1_w"), ("L0_attn_ff2_w", "ff2_w"),
            ("L0_ffn_w_gate", "wg"), ("L0_ffn_w_up", "wu"),
            ("L0_ffn_w_down", "wd"),
        ]:
            m[dst] = np.ascontiguousarray(np.asarray(inputs[src]))
        mask = np.ones((128, 2), np.float32)
        if k == 0:
            mask[:, 0] = 0.0
        if k == NCORES - 1:
            mask[:, 1] = 0.0
        sel = np.zeros((128, 8), np.float32)
        if k > 0:
            sel[:, k - 1] = 1.0
        ones = np.ones((128, 1), np.float32)
        m["smalls"] = np.ascontiguousarray(
            np.concatenate(base_cols + [mask, sel, ones,
                                        np.full((128, 1), LNEPS, np.float32)],
                           axis=1))
        assert m["smalls"].shape[1] == NSMALL, m["smalls"].shape
        m["bpcb"] = np.ascontiguousarray(bpcb)
        m["ident_f"] = ident.astype(np.float32)
        m["ident_b"] = ident.astype(ml_dtypes.bfloat16)
        m["oneover"] = np.full((128, 1), 1.0 / 1024.0, ml_dtypes.bfloat16)
        nsel = np.zeros((NS, NS * 128), np.float32)
        for n in range(NS):
            nsel[n, n * 128:(n + 1) * 128] = 1.0
        m["nsel"] = nsel.astype(ml_dtypes.bfloat16)
        in_maps.append(m)
    return in_maps


def kernel(**inputs):
    in_maps = _prep_inputs(inputs)
    nc = build_program()
    res = run_bass_kernel_spmd(nc, in_maps, core_ids=list(range(NCORES)))
    outs = []
    for k in range(NCORES):
        outs.append(res.results[k]["y_out"])
    return np.concatenate(outs, axis=0)[None, :, :]


